# version 10 (final)
# baseline (speedup 1.0000x reference)
import math
import numpy as np
import ml_dtypes
from contextlib import ExitStack
import concourse.bass as bass
import concourse.mybir as mybir
from concourse.bass_utils import run_bass_kernel_spmd

F32 = mybir.dt.float32
BF16 = mybir.dt.bfloat16
AF = mybir.ActivationFunctionType
ALU = mybir.AluOpType

NCORES = 8
B, S, D = 2, 4096, 4096
NT = 1024
TH = 512
NCH = D // 128
FH = 11008
FH4 = 2752
HB = FH // 128
HFC = 22
NH = 32
SCALE = float(128.0 ** -0.5)
SLOPES = [2.0 ** (-0.25 * (h + 1)) for h in range(NH)]
NEG = -30000.0
TILES = [(0, kt, kt) for kt in range(16)] + \
        [(1, kt, (kt if kt >= 16 else None)) for kt in range(32)]
NCAND = 32


def head_groups(h):
    if h <= 8:
        return None
    return 256 if h <= 12 else 512


NBIAS = sum((1 if head_groups(h) is None else TH // head_groups(h))
            for h in range(NH)) * len(TILES)


def _host_arrays(inputs):
    f32 = np.float32
    wqkv = np.asarray(inputs["L0_attn_w_qkv"], f32)
    bqkv = np.asarray(inputs["L0_attn_b_qkv"], f32)
    wout = np.asarray(inputs["L0_attn_w_out"], f32)
    bout = np.asarray(inputs["L0_attn_b_out"], f32)
    w1q = np.asarray(inputs["L0_ffn_w1q"], f32)
    a1 = np.asarray(inputs["L0_ffn_a1"], f32)
    b1 = np.asarray(inputs["L0_ffn_b1"], f32)
    w3q = np.asarray(inputs["L0_ffn_w3q"], f32)
    a3 = np.asarray(inputs["L0_ffn_a3"], f32)
    b3 = np.asarray(inputs["L0_ffn_b3"], f32)
    w2q = np.asarray(inputs["L0_ffn_w2q"], f32)
    a2 = np.asarray(inputs["L0_ffn_a2"], f32)
    b2 = np.asarray(inputs["L0_ffn_b2"], f32)

    d = {}
    d["qkslab"] = np.ascontiguousarray(
        wqkv[:, :8192].reshape(32, 128, 64, 128).transpose(2, 1, 0, 3))
    d["vslab"] = np.ascontiguousarray(
        wqkv[:, 8192:].reshape(32, 128, 8, 512).transpose(2, 0, 1, 3))
    d["woutslab"] = np.ascontiguousarray(
        wout.reshape(32, 128, 32, 128).transpose(2, 1, 0, 3))
    d["w1slab"] = np.ascontiguousarray(
        w1q.reshape(HB, 128, 8, 128).transpose(0, 3, 2, 1))
    d["w3slab"] = np.ascontiguousarray(
        w3q.reshape(HB, 128, 8, 128).transpose(0, 3, 2, 1))
    w2r = np.concatenate(
        [w2q.reshape(32, 128, FH4), np.zeros((32, 128, HFC * 128 - FH4), f32)], axis=2)
    d["w2slab"] = np.ascontiguousarray(
        w2r.reshape(32, 128, HFC, 128).transpose(0, 3, 2, 1))
    d["a1p"] = np.ascontiguousarray(a1.reshape(2, 32, 128).transpose(2, 1, 0).reshape(128, 64))
    d["a3p"] = np.ascontiguousarray(a3.reshape(2, 32, 128).transpose(2, 1, 0).reshape(128, 64))
    d["a2p"] = np.ascontiguousarray(a2.reshape(2, HB, 128).transpose(2, 1, 0).reshape(128, 2 * HB))
    d["b1t"] = np.ascontiguousarray(b1.T)
    d["b3t"] = np.ascontiguousarray(b3.T)
    d["b2t"] = np.ascontiguousarray(b2.T)
    d["bqkvp"] = np.ascontiguousarray(bqkv.reshape(96, 128).T)
    d["boutp"] = np.ascontiguousarray(bout.reshape(32, 128).T)
    d["bvrow"] = np.ascontiguousarray(bqkv[None, 8192:])
    p = np.arange(128, dtype=f32)[:, None]
    qq = np.arange(TH, dtype=f32)[None, :]
    d["tplain"] = np.ascontiguousarray(p - qq)
    return d


def _core_tables(j):
    jA, jB = j, 7 - j
    cols = []
    par = np.arange(128, dtype=np.float64)
    for h in range(NH):
        sl = SLOPES[h]
        w = head_groups(h)
        for (qh, kt, cand) in TILES:
            bq = jA if qh == 0 else jB
            valid = kt <= 4 * bq + 3
            k0 = kt * 128
            qb0 = bq * 512
            if w is None:
                v = sl * (k0 - qb0) if valid else NEG
                cols.append(np.full(128, v, np.float32))
            else:
                for g in range(TH // w):
                    qe = qb0 + g * w + w - 1
                    if (not valid) or (qe < k0):
                        cols.append(np.full(128, NEG, np.float32))
                    else:
                        cols.append((sl * (k0 + par - qe)).astype(np.float32))
    actbias = np.ascontiguousarray(np.stack(cols, axis=1))
    assert actbias.shape == (128, NBIAS)

    masks = np.ones((NCAND, 128, TH), np.float32)
    pi = np.arange(128)[:, None]
    qi = np.arange(TH)[None, :]
    for (qh, kt, cand) in TILES:
        if cand is None:
            continue
        bq = jA if qh == 0 else jB
        if kt // 4 == bq:
            off = kt - 4 * bq
            masks[cand] = (off * 128 + pi <= qi).astype(np.float32)
    maskarr = np.ascontiguousarray(
        masks.transpose(1, 0, 2).reshape(128, NCAND * TH)).astype(ml_dtypes.bfloat16)
    return actbias, maskarr


class Walker:
    def __init__(self, name, eng, sems, bases):
        self.name = name
        self.eng = eng
        self.sems = sems
        self.cnt = dict(bases)
        self.base = dict(bases)

    def on(self, name):
        return self.name == name and self.eng is not None

    def inc(self, sem, ins, k=1):
        if ins is not None:
            ins.then_inc(self.sems[sem], k)
        self.cnt[sem] += k

    def wait(self, engname, sem, tgt):
        if self.name == engname and self.eng is not None:
            t = max(0, int(tgt))
            if t > 0:
                self.eng.wait_ge(self.sems[sem], t)

    def wait_slack(self, engname, sem, slack):
        self.wait(engname, sem, max(self.base[sem], self.cnt[sem] - slack))


def run_phase(nc, sems, bases, walk):
    finals = []
    with nc.Block() as blk:
        for name, attr in (("POOL", "gpsimd"), ("SP", "sync"), ("PE", "tensor"),
                           ("ACT", "scalar"), ("DVE", "vector")):
            def section(eng, name=name):
                w = Walker(name, eng, sems, bases)
                walk(w)
                finals.append(w.cnt)
            getattr(blk, attr)(section)
    for c in finals[1:]:
        assert c == finals[0], ("walk divergence", finals[0], c)
    bases.clear()
    bases.update(finals[0])


def hf_segments(hb):
    segs = []
    covered = 0
    while covered < 128:
        h = 128 * hb + covered
        pos = h % FH4
        chunk, row = divmod(pos, 128)
        n = min(128 - covered, 128 - row, FH4 - pos)
        segs.append((covered, chunk, row, n, h < FH4))
        covered += n
    for (c0, ch, r0, n, f) in segs:
        assert c0 in (0, 64) and r0 in (0, 64) and n in (64, 128)
    return segs


def build_nc(debug=False):
    nc = bass.Bass("TRN2", target_bir_lowering=False)
    ikind = "ExternalOutput" if debug else "Internal"

    xin = nc.dram_tensor("xin", [D, NT], F32, kind="ExternalInput")
    xout = nc.dram_tensor("xout", [D, NT], F32, kind="ExternalOutput")
    qkslab = nc.dram_tensor("qkslab", [64, 128, 32, 128], F32, kind="ExternalInput")
    vslab = nc.dram_tensor("vslab", [8, 32, 128, 512], F32, kind="ExternalInput")
    woutslab = nc.dram_tensor("woutslab", [32, 128, 32, 128], F32, kind="ExternalInput")
    w1slab = nc.dram_tensor("w1slab", [HB, 128, 8, 128], F32, kind="ExternalInput")
    w3slab = nc.dram_tensor("w3slab", [HB, 128, 8, 128], F32, kind="ExternalInput")
    w2slab = nc.dram_tensor("w2slab", [32, 128, HFC, 128], F32, kind="ExternalInput")
    a1p_d = nc.dram_tensor("a1p", [128, 64], F32, kind="ExternalInput")
    a3p_d = nc.dram_tensor("a3p", [128, 64], F32, kind="ExternalInput")
    a2p_d = nc.dram_tensor("a2p", [128, 2 * HB], F32, kind="ExternalInput")
    b1t_d = nc.dram_tensor("b1t", [2, FH], F32, kind="ExternalInput")
    b3t_d = nc.dram_tensor("b3t", [2, FH], F32, kind="ExternalInput")
    b2t_d = nc.dram_tensor("b2t", [2, D], F32, kind="ExternalInput")
    bqkvp_d = nc.dram_tensor("bqkvp", [128, 96], F32, kind="ExternalInput")
    boutp_d = nc.dram_tensor("boutp", [128, 32], F32, kind="ExternalInput")
    bvrow_d = nc.dram_tensor("bvrow", [1, 4096], F32, kind="ExternalInput")
    tplain_d = nc.dram_tensor("tplain", [128, TH], F32, kind="ExternalInput")
    actbias_d = nc.dram_tensor("actbias", [128, NBIAS], F32, kind="ExternalInput")
    maskarr_d = nc.dram_tensor("maskarr", [128, NCAND * TH], BF16, kind="ExternalInput")

    x1_d = nc.dram_tensor("x1buf", [D, NT], F32, kind=ikind)
    x2_d = nc.dram_tensor("x2buf", [D, NT], F32, kind=ikind)
    kvloc_d = nc.dram_tensor("kvloc", [8 * NT, NT], BF16, kind="Internal")
    if debug:
        dbg_kv = nc.dram_tensor("dbgkv", [4, 4 * 2 * NT, NT], BF16, kind="ExternalOutput")
    kvg_d = [nc.dram_tensor(f"kvg{i}", [4 * 2 * NT, NT], BF16, kind="Internal")
             for i in range(4)]
    ctx_d = nc.dram_tensor("ctxbuf", [D, NT], BF16, kind=ikind)

    top = ExitStack()
    sems = {}
    for sname in ("sx", "sw", "sst", "cc1", "cc2", "cc3", "cc4",
                  "p_x", "p_sq", "a_sq", "d_st",
                  "a_rm", "d_rm", "p_bc", "d_bc",
                  "d_xn", "p_mid", "a_mid",
                  "p_g", "p_u", "a_sil", "d_prod", "d_hf", "p_a2",
                  "p_d", "d_xd",
                  "p_kq", "a_kq", "p_v", "a_v", "pl", "sxa",
                  "s_mm", "d_stt", "act_e", "d_msk", "c_mm",
                  "n_rc", "n_bc", "n_cp", "n_dn",
                  "p_o", "d_o"):
        sems[sname] = top.enter_context(nc.semaphore(sname))
    bases = {k: 0 for k in sems}

    g = {}
    for nm, shape, dt in (("bqkvp", [128, 96], F32), ("boutp", [128, 32], F32),
                          ("bqs", [128, 32], F32), ("bvrow", [1, 4096], BF16),
                          ("ones_b1", [1, 128], BF16), ("ones_bc", [128, 1], BF16),
                          ("ones_fc", [128, 1], F32), ("ones_fr", [1, 128], F32),
                          ("eps", [128, 1], F32),
                          ("sumrow", [1, NT], F32), ("sqrow", [1, NT], F32),
                          ("rrow", [1, NT], F32), ("mrrow", [1, NT], F32),
                          ("tplain", [128, TH], F32)):
        g[nm] = top.enter_context(nc.sbuf_tensor("g_" + nm, shape, dt))

    def walk_p0(w):
        for (dst, src, sem) in ((g["bqkvp"], bqkvp_d, "sx"), (g["boutp"], boutp_d, "sx"),
                                (g["tplain"], tplain_d, "sx")):
            i = w.eng.dma_start(out=dst[:, :], in_=src[:, :]) if w.on("SP") else None
            w.inc(sem, i, 16)
        i = w.eng.dma_start(out=g["bvrow"][:, :], in_=bvrow_d[:, :]) if w.on("POOL") else None
        w.inc("sw", i, 16)
        if w.on("DVE"):
            w.eng.memset(g["ones_b1"][:, :], 1.0)
            w.eng.memset(g["ones_bc"][:, :], 1.0)
            w.eng.memset(g["ones_fc"][:, :], 1.0)
            w.eng.memset(g["ones_fr"][:, :], 1.0)
            i = w.eng.memset(g["eps"][:, :], 1e-5)
        else:
            i = None
        w.inc("d_st", i)
        w.wait("ACT", "sx", w.base["sx"] + 16)
        i = (w.eng.activation(g["bqs"][:, :], g["bqkvp"][:, 0:32], AF.Identity, scale=SCALE)
             if w.on("ACT") else None)
        w.inc("a_rm", i)

    run_phase(nc, sems, bases, walk_p0)

    def do_stats(src_d, tag):
        with ExitStack() as es:
            xst = [es.enter_context(nc.sbuf_tensor(f"st_x{i}_{tag}", [128, NT], F32)) for i in range(4)]
            sqb = [es.enter_context(nc.sbuf_tensor(f"st_q{i}_{tag}", [128, NT], F32)) for i in range(2)]
            sum_ps = es.enter_context(nc.psum_tensor(f"st_sum_{tag}", [1, NT], F32))
            sq_ps = es.enter_context(nc.psum_tensor(f"st_sq_{tag}", [1, NT], F32))

            def walk(w):
                for c in range(NCH):
                    w.wait_slack("SP", "a_sq", 3)
                    w.wait_slack("SP", "p_x", 3)
                    i = (w.eng.dma_start(out=xst[c % 4][:, :], in_=src_d[128 * c:128 * (c + 1), :])
                         if w.on("SP") else None)
                    w.inc("sx", i, 16)
                    w.wait("ACT", "sx", w.cnt["sx"])
                    w.wait_slack("ACT", "p_sq", 1)
                    i = (w.eng.activation(sqb[c % 2][:, :], xst[c % 4][:, :], AF.Square)
                         if w.on("ACT") else None)
                    w.inc("a_sq", i)
                    w.wait("PE", "sx", w.cnt["sx"])
                    if w.on("PE"):
                        w.eng.matmul(sum_ps[0:1, 0:TH], g["ones_fc"][:, 0:1], xst[c % 4][:, 0:TH],
                                     start=(c == 0), stop=(c == NCH - 1))
                        i = w.eng.matmul(sum_ps[0:1, TH:NT], g["ones_fc"][:, 0:1], xst[c % 4][:, TH:NT],
                                         start=(c == 0), stop=(c == NCH - 1))
                    else:
                        i = None
                    w.inc("p_x", i)
                    w.wait("PE", "a_sq", w.cnt["a_sq"])
                    if w.on("PE"):
                        w.eng.matmul(sq_ps[0:1, 0:TH], g["ones_fc"][:, 0:1], sqb[c % 2][:, 0:TH],
                                     start=(c == 0), stop=(c == NCH - 1))
                        i = w.eng.matmul(sq_ps[0:1, TH:NT], g["ones_fc"][:, 0:1], sqb[c % 2][:, TH:NT],
                                         start=(c == 0), stop=(c == NCH - 1))
                    else:
                        i = None
                    w.inc("p_sq", i)
                w.wait("DVE", "p_x", w.cnt["p_x"])
                w.wait("DVE", "p_sq", w.cnt["p_sq"])
                if w.on("DVE"):
                    w.eng.tensor_copy(g["sumrow"][0:1, :], sum_ps[0:1, :])
                    i = w.eng.tensor_copy(g["sqrow"][0:1, :], sq_ps[0:1, :])
                else:
                    i = None
                w.inc("d_st", i)

            run_phase(nc, sems, bases, walk)

    def make_rowmath(t, rb, mrb):
        def emit(w):
            w.wait("ACT", "d_st", w.cnt["d_st"])
            if w.on("ACT"):
                w.eng.activation(t["meanr"], g["sumrow"][0:1, :], AF.Identity, scale=1.0 / D)
                i = w.eng.activation(t["m2"], t["meanr"], AF.Square)
            else:
                i = None
            w.inc("a_rm", i)
            w.wait("DVE", "a_rm", w.cnt["a_rm"])
            i = (w.eng.scalar_tensor_tensor(t["varr"], g["sqrow"][0:1, :], 1.0 / D,
                                            t["m2"], ALU.mult, ALU.subtract)
                 if w.on("DVE") else None)
            w.inc("d_rm", i)
            w.wait("ACT", "d_rm", w.cnt["d_rm"])
            if w.on("ACT"):
                w.eng.activation(t["lnv"], t["varr"], AF.Ln, bias=g["eps"][0:1, 0:1])
                i = w.eng.activation(g["rrow"][0:1, :], t["lnv"], AF.Exp, scale=-0.5)
            else:
                i = None
            w.inc("a_rm", i)
            w.wait("DVE", "a_rm", w.cnt["a_rm"])
            i = (w.eng.tensor_mul(g["mrrow"][0:1, :], t["meanr"], g["rrow"][0:1, :])
                 if w.on("DVE") else None)
            w.inc("d_rm", i)
            for (row, dst) in ((g["rrow"], rb), (g["mrrow"], mrb)):
                for th in range(2):
                    w.wait("PE", "d_rm", w.cnt["d_rm"])
                    w.wait("PE", "d_bc", w.cnt["d_bc"])
                    i = (w.eng.matmul(t["bc_ps"][:, :], g["ones_fr"][0:1, :],
                                      row[0:1, th * TH:(th + 1) * TH], start=True, stop=True)
                         if w.on("PE") else None)
                    w.inc("p_bc", i)
                    w.wait("DVE", "p_bc", w.cnt["p_bc"])
                    i = (w.eng.tensor_copy(dst[:, th * TH:(th + 1) * TH], t["bc_ps"][:, :])
                         if w.on("DVE") else None)
                    w.inc("d_bc", i)
        return emit

    def emit_ffn(src_d, dst_d, tag, final=False, fuse_stats=False):
        es = ExitStack()
        sb = lambda nm, sh, dt: es.enter_context(nc.sbuf_tensor(f"{nm}_{tag}", sh, dt))
        rb = sb("f_rb", [128, NT], F32)
        mrb = sb("f_mrb", [128, NT], F32)
        xst = [sb(f"f_x{i}", [128, NT], F32) for i in range(3)]
        xnb = [sb(f"f_xn{i}", [128, NT], BF16) for i in range(2)]
        xf = [sb(f"f_xf{i}", [128, NT], BF16) for i in range(8)]
        hf = [sb(f"f_hf{i}", [128, NT], BF16) for i in range(HFC)]
        a1p = sb("f_a1p", [128, 64], BF16)
        a3p = sb("f_a3p", [128, 64], BF16)
        a2p = sb("f_a2p", [128, 2 * HB], BF16)
        b1t = sb("f_b1t", [2, FH], BF16)
        b3t = sb("f_b3t", [2, FH], BF16)
        b2t = sb("f_b2t", [2, D], BF16)
        m1sb = sb("f_m1", [2, NT], BF16)
        m3sb = sb("f_m3", [2, NT], BF16)
        m2sb = sb("f_m2", [2, NT], BF16)
        w1b = [sb(f"f_w1{i}", [128, 8 * 128], BF16) for i in range(2)]
        w3b = [sb(f"f_w3{i}", [128, 8 * 128], BF16) for i in range(2)]
        w2b = [sb(f"f_w2{i}", [128, HFC * 128], BF16) for i in range(2)]
        gsb = [[sb(f"f_g{t}{i}", [128, TH], BF16) for i in range(2)] for t in range(2)]
        psb = [[sb(f"f_p{t}{i}", [128, TH], BF16) for i in range(2)] for t in range(2)]
        osb = [sb(f"f_o{i}", [128, NT], F32) for i in range(2)]
        tmpa = osb[0]

        with ExitStack() as es1:
            rmt = {"meanr": osb[1][0:1, :], "m2": xst[0][0:1, :],
                   "varr": xst[1][0:1, :], "lnv": xst[2][0:1, :]}
            rmt["bc_ps"] = es1.enter_context(nc.psum_tensor(f"rm_bc_{tag}", [128, TH], F32))
            rme = make_rowmath(rmt, rb, mrb)

            def walk_pre(w):
                for (dst, src) in ((a1p, a1p_d), (a3p, a3p_d), (a2p, a2p_d),
                                   (b1t, b1t_d), (b3t, b3t_d), (b2t, b2t_d)):
                    i = w.eng.dma_start(out=dst[:, :], in_=src[:, :]) if w.on("POOL") else None
                    w.inc("sw", i, 16)
                if w.on("DVE"):
                    w.eng.memset(hf[HFC - 1][64:128, :], 0.0)
                rme(w)
            run_phase(nc, sems, bases, walk_pre)

        with ExitStack() as es2:
            m1ps = [es2.enter_context(nc.psum_tensor(f"f_m1ps{t}_{tag}", [2, TH], F32)) for t in range(2)]
            m3ps = [es2.enter_context(nc.psum_tensor(f"f_m3ps{t}_{tag}", [2, TH], F32)) for t in range(2)]

            def walk_b(w):
                for c in range(NCH):
                    w.wait_slack("SP", "d_xn", 2)
                    i = (w.eng.dma_start(out=xst[c % 3][:, :], in_=src_d[128 * c:128 * (c + 1), :])
                         if w.on("SP") else None)
                    w.inc("sx", i, 16)
                    w.wait("DVE", "sx", w.cnt["sx"])
                    w.wait_slack("DVE", "p_mid", 1)
                    if w.on("DVE"):
                        w.eng.tensor_mul(tmpa[:, :], xst[c % 3][:, :], rb[:, :])
                        w.eng.tensor_sub(xnb[c % 2][:, :], tmpa[:, :], mrb[:, :])
                        if c < 8:
                            i = w.eng.tensor_copy(xf[c][:, :], xnb[c % 2][:, :])
                        else:
                            i = w.eng.tensor_add(xf[c % 8][:, :], xf[c % 8][:, :], xnb[c % 2][:, :])
                    else:
                        i = None
                    w.inc("d_xn", i)
                    w.wait("PE", "d_xn", w.cnt["d_xn"])
                    if w.on("PE"):
                        for t in range(2):
                            w.eng.matmul(m1ps[t][:, :], a1p[:, 2 * c:2 * c + 2],
                                         xnb[c % 2][:, t * TH:(t + 1) * TH],
                                         start=(c == 0), stop=(c == NCH - 1))
                        for t in range(2):
                            i = w.eng.matmul(m3ps[t][:, :], a3p[:, 2 * c:2 * c + 2],
                                             xnb[c % 2][:, t * TH:(t + 1) * TH],
                                             start=(c == 0), stop=(c == NCH - 1))
                    else:
                        i = None
                    w.inc("p_mid", i)
                w.wait("ACT", "p_mid", w.cnt["p_mid"])
                w.wait("ACT", "sw", w.cnt["sw"])
                if w.on("ACT"):
                    for t in range(2):
                        w.eng.activation(m1sb[0:2, t * TH:(t + 1) * TH], m1ps[t][:, :],
                                         AF.Identity, scale=0.5)
                    for t in range(2):
                        i = w.eng.activation(m3sb[0:2, t * TH:(t + 1) * TH], m3ps[t][:, :],
                                             AF.Identity, scale=0.5)
                else:
                    i = None
                w.inc("a_mid", i)
            run_phase(nc, sems, bases, walk_b)

        with ExitStack() as es3:
            gps = [es3.enter_context(nc.psum_tensor(f"f_gps{t}_{tag}", [128, TH], F32)) for t in range(2)]
            ups = [es3.enter_context(nc.psum_tensor(f"f_ups{t}_{tag}", [128, TH], F32)) for t in range(2)]
            m2ps = [es3.enter_context(nc.psum_tensor(f"f_m2ps{t}_{tag}", [2, TH], F32)) for t in range(2)]

            def walk_c(w):
                for hb in range(HB):
                    for (buf, slab) in ((w1b, w1slab), (w3b, w3slab)):
                        w.wait_slack("POOL", "p_u", 2)
                        i = (w.eng.dma_start(out=buf[hb % 2][:, :],
                                             in_=slab[hb].rearrange("p cc h -> p (cc h)"))
                             if w.on("POOL") else None)
                        w.inc("sw", i, 16)
                    if hb > 0:
                        for t in range(2):
                            w.wait("PE", "d_prod", w.cnt["d_prod"])
                            i = (w.eng.matmul(m2ps[t][:, :], a2p[:, 2 * (hb - 1):2 * hb],
                                              psb[t][(hb - 1) % 2][:, :],
                                              start=(hb == 1), stop=False)
                                 if w.on("PE") else None)
                            w.inc("p_a2", i)
                    w.wait("PE", "sw", w.cnt["sw"])
                    if hb == 0:
                        w.wait("PE", "a_mid", w.base["a_mid"])
                    for t in range(2):
                        w.wait_slack("PE", "a_sil", 1 - t)
                        if w.on("PE"):
                            for cc in range(8):
                                w.eng.matmul(gps[t][:, :], w1b[hb % 2][:, 128 * cc:128 * (cc + 1)],
                                             xf[cc][:, t * TH:(t + 1) * TH],
                                             start=(cc == 0), stop=False)
                            i = w.eng.matmul(gps[t][:, :], b1t[0:2, 128 * hb:128 * (hb + 1)],
                                             m1sb[0:2, t * TH:(t + 1) * TH], start=False, stop=True)
                        else:
                            i = None
                        w.inc("p_g", i)
                    for t in range(2):
                        w.wait_slack("PE", "d_prod", 1 - t)
                        if w.on("PE"):
                            for cc in range(8):
                                w.eng.matmul(ups[t][:, :], w3b[hb % 2][:, 128 * cc:128 * (cc + 1)],
                                             xf[cc][:, t * TH:(t + 1) * TH],
                                             start=(cc == 0), stop=False)
                            i = w.eng.matmul(ups[t][:, :], b3t[0:2, 128 * hb:128 * (hb + 1)],
                                             m3sb[0:2, t * TH:(t + 1) * TH], start=False, stop=True)
                        else:
                            i = None
                        w.inc("p_u", i)
                    for t in range(2):
                        w.wait("ACT", "p_g", w.base["p_g"] + 2 * hb + t + 1)
                        w.wait_slack("ACT", "d_hf", 3)
                        i = (w.eng.activation(gsb[t][hb % 2][:, :], gps[t][:, :], AF.Silu)
                             if w.on("ACT") else None)
                        w.inc("a_sil", i)
                        w.wait("DVE", "a_sil", w.cnt["a_sil"])
                        w.wait("DVE", "p_u", w.base["p_u"] + 2 * hb + t + 1)
                        w.wait_slack("DVE", "p_a2", 1)
                        i = (w.eng.tensor_mul(psb[t][hb % 2][:, :], gsb[t][hb % 2][:, :], ups[t][:, :])
                             if w.on("DVE") else None)
                        w.inc("d_prod", i)
                        if w.on("DVE"):
                            for (c0, chk, r0, n, first) in hf_segments(hb):
                                sl_src = psb[t][hb % 2][c0:c0 + n, :]
                                sl_dst = hf[chk][r0:r0 + n, t * TH:(t + 1) * TH]
                                if first:
                                    i = w.eng.tensor_copy(sl_dst, sl_src)
                                elif c0 == r0:
                                    i = w.eng.tensor_add(sl_dst, sl_dst, sl_src)
                                else:
                                    scr = gsb[t][hb % 2][r0:r0 + n, :]
                                    w.eng.tensor_copy(scr, sl_src)
                                    i = w.eng.tensor_add(sl_dst, sl_dst, scr)
                        w.inc("d_hf", i)
                for t in range(2):
                    w.wait("PE", "d_prod", w.cnt["d_prod"])
                    i = (w.eng.matmul(m2ps[t][:, :], a2p[:, 2 * (HB - 1):2 * HB],
                                      psb[t][(HB - 1) % 2][:, :], start=False, stop=True)
                         if w.on("PE") else None)
                    w.inc("p_a2", i)
                w.wait("ACT", "p_a2", w.cnt["p_a2"])
                if w.on("ACT"):
                    for t in range(2):
                        i = w.eng.activation(m2sb[0:2, t * TH:(t + 1) * TH], m2ps[t][:, :],
                                             AF.Identity, scale=0.5)
                else:
                    i = None
                w.inc("a_mid", i)
            run_phase(nc, sems, bases, walk_c)

        with ExitStack() as es4:
            ops = [es4.enter_context(nc.psum_tensor(f"f_ops{i}_{tag}", [128, TH], F32)) for i in range(4)]
            if fuse_stats:
                st_sum = es4.enter_context(nc.psum_tensor(f"f_stsum_{tag}", [1, NT], F32))
                st_sq = es4.enter_context(nc.psum_tensor(f"f_stsq_{tag}", [1, NT], F32))
                sqs = [rb, mrb]

            def emit_dstats(w, dm):
                for th in range(2):
                    cl = slice(th * TH, (th + 1) * TH)
                    i = (w.eng.matmul(st_sum[0:1, cl], g["ones_fc"][:, 0:1], osb[dm % 2][:, cl],
                                      start=(dm == 0), stop=(dm == NCH - 1))
                         if w.on("PE") else None)
                w.inc("p_x", i)
                for th in range(2):
                    cl = slice(th * TH, (th + 1) * TH)
                    i = (w.eng.matmul(st_sq[0:1, cl], g["ones_fc"][:, 0:1], sqs[dm % 2][:, cl],
                                      start=(dm == 0), stop=(dm == NCH - 1))
                         if w.on("PE") else None)
                w.inc("p_sq", i)

            def walk_d(w):
                for dm in range(NCH):
                    if fuse_stats and dm > 1:
                        w.wait_slack("PE", "a_sq", 1)
                        emit_dstats(w, dm - 2)
                    w.wait_slack("POOL", "p_d", 2)
                    i = (w.eng.dma_start(out=w2b[dm % 2][:, :],
                                         in_=w2slab[dm].rearrange("p hc d -> p (hc d)"))
                         if w.on("POOL") else None)
                    w.inc("sw", i, 16)
                    w.wait_slack("SP", "d_xd", 4)
                    i = (w.eng.dma_start(out=xst[dm % 3][:, :], in_=src_d[128 * dm:128 * (dm + 1), :])
                         if w.on("SP") else None)
                    w.inc("sx", i, 16)
                    w.wait("PE", "sw", w.cnt["sw"])
                    if dm == 0:
                        w.wait("PE", "d_hf", w.base["d_hf"])
                        w.wait("PE", "a_mid", w.base["a_mid"])
                    for t in range(2):
                        slot = (dm % 2) * 2 + t
                        w.wait_slack("PE", "d_xd", 3)
                        if w.on("PE"):
                            for hc in range(HFC):
                                w.eng.matmul(ops[slot][:, :], w2b[dm % 2][:, 128 * hc:128 * (hc + 1)],
                                             hf[hc][:, t * TH:(t + 1) * TH],
                                             start=(hc == 0), stop=False)
                            i = w.eng.matmul(ops[slot][:, :], b2t[0:2, 128 * dm:128 * (dm + 1)],
                                             m2sb[0:2, t * TH:(t + 1) * TH], start=False, stop=True)
                        else:
                            i = None
                        w.inc("p_d", i)
                        w.wait("DVE", "p_d", w.cnt["p_d"])
                        w.wait("DVE", "sx", w.cnt["sx"])
                        w.wait_slack("DVE", "sst", 16)
                        if fuse_stats:
                            w.wait_slack("DVE", "p_x", 0)
                        i = (w.eng.scalar_tensor_tensor(
                            osb[dm % 2][:, t * TH:(t + 1) * TH], ops[slot][:, :], 0.5,
                            xst[dm % 3][:, t * TH:(t + 1) * TH], ALU.mult, ALU.add)
                            if w.on("DVE") else None)
                        w.inc("d_xd", i)
                    if fuse_stats:
                        w.wait("ACT", "d_xd", w.cnt["d_xd"])
                        w.wait_slack("ACT", "p_sq", 0)
                        i = (w.eng.activation(sqs[dm % 2][:, :], osb[dm % 2][:, :], AF.Square)
                             if w.on("ACT") else None)
                        w.inc("a_sq", i)
                    w.wait("SP", "d_xd", w.cnt["d_xd"])
                    i = (w.eng.dma_start(out=dst_d[128 * dm:128 * (dm + 1), :], in_=osb[dm % 2][:, :])
                         if w.on("SP") else None)
                    w.inc("sst", i, 16)
                if fuse_stats:
                    w.wait("PE", "a_sq", w.cnt["a_sq"])
                    emit_dstats(w, NCH - 2)
                    emit_dstats(w, NCH - 1)
                    w.wait("DVE", "p_x", w.cnt["p_x"])
                    w.wait("DVE", "p_sq", w.cnt["p_sq"])
                    if w.on("DVE"):
                        w.eng.tensor_copy(g["sumrow"][0:1, :], st_sum[0:1, :])
                        i = w.eng.tensor_copy(g["sqrow"][0:1, :], st_sq[0:1, :])
                    else:
                        i = None
                    w.inc("d_st", i)
                if final:
                    w.wait("SP", "sst", w.cnt["sst"])
            run_phase(nc, sems, bases, walk_d)
        es.close()

    do_stats(xin, "s0")
    emit_ffn(xin, x1_d, "f1", fuse_stats=True)

    es_att = ExitStack()
    qsb = es_att.enter_context(nc.sbuf_tensor("at_q", [128, NH * NT], BF16))

    es_prj = ExitStack()
    rb2 = es_prj.enter_context(nc.sbuf_tensor("at_rb", [128, NT], F32))
    mrb2 = es_prj.enter_context(nc.sbuf_tensor("at_mrb", [128, NT], F32))
    tmpa2 = es_prj.enter_context(nc.sbuf_tensor("at_tmpa", [128, NT], F32))
    xst2 = [es_prj.enter_context(nc.sbuf_tensor(f"at_x{i}", [128, NT], F32)) for i in range(3)]
    xn2 = es_prj.enter_context(nc.sbuf_tensor("at_xn2", [128, NCH * NT], BF16))
    qkw = [es_prj.enter_context(nc.sbuf_tensor(f"at_qkw{i}", [128, 32 * 128], BF16)) for i in range(2)]
    kqs = [es_prj.enter_context(nc.sbuf_tensor(f"at_kqs{i}", [128, NT], BF16)) for i in range(2)]
    vwb = [es_prj.enter_context(nc.sbuf_tensor(f"at_vw{i}", [128, TH], BF16)) for i in range(4)]
    vsb = [es_prj.enter_context(nc.sbuf_tensor(f"at_vs{i}", [128, TH], BF16)) for i in range(2)]

    with ExitStack() as esr:
        rmt2 = {"meanr": tmpa2[0:1, :], "m2": xst2[0][0:1, :],
                "varr": xst2[1][0:1, :], "lnv": rb2[0:1, :]}
        rmt2["bc_ps"] = esr.enter_context(nc.psum_tensor("rm2_bc", [128, TH], F32))
        rme2 = make_rowmath(rmt2, rb2, mrb2)

        def walk_xn2(w):
            rme2(w)
            for c in range(NCH):
                q = "SP" if c % 2 == 0 else "ACT"
                qsem = "sx" if q == "SP" else "sxa"
                if c <= 1:
                    w.wait(q, "d_bc", w.cnt["d_bc"])
                w.wait_slack(q, "d_xn", 2)
                w.wait_slack(q, "pl", 2)
                i = (w.eng.dma_start(out=xst2[c % 3][:, :], in_=x1_d[128 * c:128 * (c + 1), :])
                     if w.on(q) else None)
                w.inc(qsem, i, 16)
                w.wait("DVE", qsem, w.cnt[qsem])
                if w.on("DVE"):
                    w.eng.tensor_mul(tmpa2[:, 0:TH], xst2[c % 3][:, 0:TH], rb2[:, 0:TH])
                    i = w.eng.tensor_sub(xn2[:, c * NT:c * NT + TH], tmpa2[:, 0:TH], mrb2[:, 0:TH])
                else:
                    i = None
                w.inc("d_xn", i)
                w.wait("POOL", qsem, w.cnt[qsem])
                if w.on("POOL"):
                    w.eng.tensor_mul(tmpa2[:, TH:NT], xst2[c % 3][:, TH:NT], rb2[:, TH:NT])
                    i = w.eng.tensor_sub(xn2[:, c * NT + TH:(c + 1) * NT], tmpa2[:, TH:NT], mrb2[:, TH:NT])
                else:
                    i = None
                w.inc("pl", i)
        run_phase(nc, sems, bases, walk_xn2)

    def do_fm_proj(fblk_base, bias_tile, bias_off, scale, out_ap_fn, store_fn, tag):
        with ExitStack() as es:
            pps = [es.enter_context(nc.psum_tensor(f"pj_{tag}{i}", [128, TH], F32)) for i in range(4)]

            def walk(w):
                for f in range(NCH):
                    w.wait_slack("POOL", "p_kq", 2)
                    i = (w.eng.dma_start(out=qkw[f % 2][:, :],
                                         in_=qkslab[fblk_base + f].rearrange("p cc h -> p (cc h)"))
                         if w.on("POOL") else None)
                    w.inc("sw", i, 16)
                    w.wait("PE", "sw", w.cnt["sw"])
                    if f == 0:
                        w.wait("PE", "d_xn", w.base["d_xn"])
                        w.wait("PE", "pl", w.base["pl"])
                    for t in range(2):
                        slot = (f % 2) * 2 + t
                        w.wait_slack("PE", "a_kq", 2)
                        if w.on("PE"):
                            for cc in range(NCH):
                                i = w.eng.matmul(pps[slot][:, :], qkw[f % 2][:, 128 * cc:128 * (cc + 1)],
                                                 xn2[:, cc * NT + t * TH: cc * NT + (t + 1) * TH],
                                                 start=(cc == 0), stop=(cc == NCH - 1))
                        else:
                            i = None
                        w.inc("p_kq", i)
                        w.wait("ACT", "p_kq", w.cnt["p_kq"])
                        w.wait_slack("ACT", "sst", 16)
                        i = (w.eng.activation(out_ap_fn(f, t), pps[slot][:, :], AF.Identity,
                                              bias=bias_tile[:, bias_off + f:bias_off + f + 1],
                                              scale=scale)
                             if w.on("ACT") else None)
                        w.inc("a_kq", i)
                    store_fn(w, f)
            run_phase(nc, sems, bases, walk)

    def k_out(f, t):
        return kqs[f % 2][:, t * TH:(t + 1) * TH]

    def k_store(w, f):
        w.wait("SP", "a_kq", w.cnt["a_kq"])
        row = 2048 * (f // 8) + 128 * (f % 8)
        i = (w.eng.dma_start(out=kvloc_d[row:row + 128, :], in_=kqs[f % 2][:, :])
             if w.on("SP") else None)
        w.inc("sst", i, 16)

    do_fm_proj(32, g["bqkvp"], 32, 1.0, k_out, k_store, "k")

    with ExitStack() as esv:
        vps = [esv.enter_context(nc.psum_tensor(f"pj_v{i}", [128, TH], F32)) for i in range(8)]

        def walk_vproj(w):
            for fb in range(8):
                for cc in range(NCH):
                    w.wait_slack("POOL", "p_v", 3)
                    i = (w.eng.dma_start(out=vwb[cc % 4][:, :], in_=vslab[fb, cc])
                         if w.on("POOL") else None)
                    w.inc("sw", i, 16)
                    w.wait("PE", "sw", w.cnt["sw"])
                    ins_pv = None
                    if cc == 0:
                        w.wait("PE", "a_v", w.base["a_v"] + 8 * fb)
                        if w.on("PE"):
                            for t in range(8):
                                w.eng.matmul(vps[t][:, :], g["ones_b1"][0:1, :],
                                             g["bvrow"][0:1, fb * TH:(fb + 1) * TH],
                                             start=True, stop=False)
                                ins_pv = w.eng.matmul(
                                    vps[t][:, :],
                                    xn2[:, cc * NT + t * 128: cc * NT + (t + 1) * 128],
                                    vwb[cc % 4][:, :], start=False, stop=False)
                    elif w.on("PE"):
                        for t in range(8):
                            ins_pv = w.eng.matmul(
                                vps[t][:, :],
                                xn2[:, cc * NT + t * 128: cc * NT + (t + 1) * 128],
                                vwb[cc % 4][:, :], start=False, stop=(cc == NCH - 1))
                    w.inc("p_v", ins_pv)
                for t in range(8):
                    w.wait("ACT", "p_v", w.cnt["p_v"])
                    w.wait_slack("ACT", "sst", 16)
                    i = (w.eng.activation(vsb[t % 2][:, :], vps[t][:, :], AF.Identity)
                         if w.on("ACT") else None)
                    w.inc("a_v", i)
                    w.wait("SP", "a_v", w.cnt["a_v"])
                    vrow = 2048 * (fb // 2) + 1024 + 128 * t
                    i = (w.eng.dma_start(out=kvloc_d[vrow:vrow + 128,
                                                     (fb % 2) * TH:(fb % 2) * TH + TH],
                                         in_=vsb[t % 2][:, :])
                         if w.on("SP") else None)
                    w.inc("sst", i, 16)
        run_phase(nc, sems, bases, walk_vproj)

    def walk_ag(w):
        w.wait("POOL", "sst", w.cnt["sst"])
        for i4 in range(4):
            if w.on("POOL"):
                i = w.eng.collective_compute(
                    "AllGather", ALU.bypass,
                    replica_groups=[[0, 1, 2, 3], [4, 5, 6, 7]],
                    ins=[kvloc_d[2048 * i4:2048 * (i4 + 1), :].opt()],
                    outs=[kvg_d[i4].ap().opt()])
                i.then_inc(sems[f"cc{i4 + 1}"], 1)
            w.cnt[f"cc{i4 + 1}"] += 1
    run_phase(nc, sems, bases, walk_ag)

    def q_out(f, t):
        return qsb[:, f * NT + t * TH: f * NT + (t + 1) * TH]

    def q_store(w, f):
        pass

    do_fm_proj(0, g["bqs"], 0, SCALE, q_out, q_store, "q")
    es_prj.close()

    es_core = ExitStack()
    actb = es_core.enter_context(nc.sbuf_tensor("at_actb", [128, NBIAS], F32))
    maskb = es_core.enter_context(nc.sbuf_tensor("at_mask", [128, NCAND * TH], BF16))
    khb = [es_core.enter_context(nc.sbuf_tensor(f"at_kh{i}", [128, 32 * 128], BF16)) for i in range(2)]
    vtb = [es_core.enter_context(nc.sbuf_tensor(f"at_vt{i}", [128, 32 * 128], BF16)) for i in range(2)]
    prb = [es_core.enter_context(nc.sbuf_tensor(f"at_pr{i}", [128, TH], BF16)) for i in range(4)]
    tmf = [es_core.enter_context(nc.sbuf_tensor(f"at_tm{i}", [128, TH], F32)) for i in range(2)]
    invr = es_core.enter_context(nc.sbuf_tensor("at_invr", [1, NT], F32))
    invb = es_core.enter_context(nc.sbuf_tensor("at_invb", [128, NT], F32))
    ctxs = [es_core.enter_context(nc.sbuf_tensor(f"at_cx{i}", [128, NT], BF16)) for i in range(2)]

    with ExitStack() as esc:
        sps = [esc.enter_context(nc.psum_tensor(f"at_sps{i}", [128, TH], F32)) for i in range(2)]
        cxp = [esc.enter_context(nc.psum_tensor(f"at_cxp{i}", [128, NT], F32)) for i in range(2)]
        smp = esc.enter_context(nc.psum_tensor("at_smp", [128, NT], F32))

        def kh_load(w, h, buf):
            i4, hh = divmod(h, 8)
            for m in range(4):
                rows = kvg_d[i4][2048 * m + 128 * hh: 2048 * m + 128 * (hh + 1), :]
                i = (w.eng.dma_start(out=buf[:, (4 * m) * 128:(4 * m + 4) * 128],
                                     in_=rows[:, 0:TH]) if w.on("SP") else None)
                w.inc("sx", i, 16)
                b7 = 7 - m
                i = (w.eng.dma_start(out=buf[:, (4 * b7) * 128:(4 * b7 + 4) * 128],
                                     in_=rows[:, TH:NT]) if w.on("SP") else None)
                w.inc("sx", i, 16)

        def vt_load(w, h, buf):
            i4, hh = divmod(h, 8)
            for m in range(4):
                for half in range(2):
                    b = m if half == 0 else 7 - m
                    src = kvg_d[i4][2048 * m + 1024 + half * TH: 2048 * m + 1024 + (half + 1) * TH,
                                    128 * hh:128 * (hh + 1)].rearrange("(sub p) d -> p sub d", p=128)
                    dst = buf[:, (4 * b) * 128:(4 * b + 4) * 128].rearrange(
                        "p (sub d) -> p sub d", d=128)
                    i = (w.eng.dma_start(out=dst, in_=src) if w.on("SP") else None)
                    w.inc("sx", i, 16)

        def emit_ctx_sum(w, h, p):
            i_t, qh, kt, cand, act_tgt, msk_tgt = p
            if cand is not None:
                w.wait("PE", "d_msk", msk_tgt)
            else:
                w.wait("PE", "act_e", act_tgt)
            if kt == 0:
                w.wait("PE", "n_dn", w.base["n_dn"] + max(0, h - 1))
                w.wait("PE", "n_cp", w.base["n_cp"] + h)
            last = 15 if qh == 0 else 31
            if w.on("PE"):
                w.eng.matmul(cxp[h % 2][:, qh * TH:(qh + 1) * TH],
                             vtb[h % 2][:, kt * 128:(kt + 1) * 128],
                             prb[i_t % 4][:, :], start=(kt == 0), stop=(kt == last))
                ins = w.eng.matmul(smp[0:1, qh * TH:(qh + 1) * TH],
                                   g["ones_bc"][:, 0:1],
                                   prb[i_t % 4][:, :], start=(kt == 0), stop=(kt == last))
            else:
                ins = None
            w.inc("c_mm", ins)

        def walk_core(w):
            if w.on("SP"):
                w.eng.wait_ge(sems["cc1"], 1)
            if debug:
                for dbgi in range(4):
                    if w.on("SP"):
                        w.eng.wait_ge(sems[f"cc{dbgi + 1}"], 1)
                        i = w.eng.dma_start(out=dbg_kv[dbgi], in_=kvg_d[dbgi][:, :])
                    else:
                        i = None
                    w.inc("sst", i, 16)
            i = (w.eng.dma_start(out=actb[:, :], in_=actbias_d[:, :]) if w.on("SP") else None)
            w.inc("sx", i, 16)
            i = (w.eng.dma_start(out=maskb[:, :], in_=maskarr_d[:, :]) if w.on("SP") else None)
            w.inc("sx", i, 16)
            kh_load(w, 0, khb[0])
            vt_load(w, 0, vtb[0])
            head_ready = {0: w.cnt["sx"]}

            bias_idx = 0
            for h in range(NH):
                sl = SLOPES[h]
                wgrp = head_groups(h)
                if h + 1 < NH:
                    w.wait("SP", "c_mm", w.cnt["c_mm"])
                    if (h + 1) % 8 == 0 and w.on("SP"):
                        w.eng.wait_ge(sems[f"cc{(h + 1) // 8 + 1}"], 1)
                    kh_load(w, h + 1, khb[(h + 1) % 2])
                    vt_load(w, h + 1, vtb[(h + 1) % 2])
                    head_ready[h + 1] = w.cnt["sx"]
                pend = []
                for i_t, (qh, kt, cand) in enumerate(TILES):
                    if i_t == 0:
                        w.wait("PE", "sx", head_ready[h])
                        w.wait("ACT", "sx", head_ready[0])
                        w.wait("DVE", "sx", head_ready[0])
                    w.wait_slack("PE", "act_e", 1)
                    ins = (w.eng.matmul(sps[i_t % 2][:, :],
                                        khb[h % 2][:, kt * 128:(kt + 1) * 128],
                                        qsb[:, h * NT + qh * TH: h * NT + (qh + 1) * TH],
                                        start=True, stop=True)
                           if w.on("PE") else None)
                    w.inc("s_mm", ins)
                    if wgrp is None:
                        w.wait("DVE", "s_mm", w.cnt["s_mm"])
                        w.wait_slack("DVE", "act_e", 1)
                        if w.on("DVE"):
                            w.eng.scalar_tensor_tensor(tmf[i_t % 2][:, :], g["tplain"][:, :],
                                                       float(sl), sps[i_t % 2][:, :],
                                                       ALU.mult, ALU.add)
                            ins = w.eng.tensor_scalar(tmf[i_t % 2][:, :], tmf[i_t % 2][:, :],
                                                      actb[:, bias_idx:bias_idx + 1], 80.0,
                                                      ALU.add, ALU.min)
                        else:
                            ins = None
                        w.inc("d_stt", ins)
                        w.wait("ACT", "d_stt", w.cnt["d_stt"])
                        w.wait("ACT", "c_mm", w.cnt["c_mm"])
                        ins = (w.eng.activation(prb[i_t % 4][:, :], tmf[i_t % 2][:, :], AF.Exp)
                               if w.on("ACT") else None)
                        w.inc("act_e", ins)
                        bias_idx += 1
                    else:
                        w.wait("ACT", "s_mm", w.cnt["s_mm"])
                        w.wait("ACT", "c_mm", w.cnt["c_mm"])
                        ngr = TH // wgrp
                        for gci in range(ngr):
                            cl = slice(gci * wgrp, (gci + 1) * wgrp)
                            ins = (w.eng.activation(prb[i_t % 4][:, cl], sps[i_t % 2][:, cl], AF.Exp,
                                                    bias=actb[:, bias_idx:bias_idx + 1])
                                   if w.on("ACT") else None)
                            if gci == ngr - 1:
                                w.inc("act_e", ins)
                            bias_idx += 1
                    if cand is not None:
                        w.wait("DVE", "act_e", w.cnt["act_e"])
                        ins = (w.eng.tensor_mul(prb[i_t % 4][:, :], prb[i_t % 4][:, :],
                                                maskb[:, cand * TH:(cand + 1) * TH])
                               if w.on("DVE") else None)
                        w.inc("d_msk", ins)
                    pend.append((i_t, qh, kt, cand, w.cnt["act_e"], w.cnt["d_msk"]))
                    if len(pend) > 3:
                        emit_ctx_sum(w, h, pend.pop(0))
                for p in pend:
                    emit_ctx_sum(w, h, p)
                w.wait("DVE", "c_mm", w.cnt["c_mm"])
                ins = (w.eng.reciprocal(invr[0:1, :], smp[0:1, :]) if w.on("DVE") else None)
                w.inc("n_rc", ins)
                w.wait("PE", "n_rc", w.cnt["n_rc"])
                if w.on("PE"):
                    for t in range(2):
                        ins = w.eng.matmul(smp[:, t * TH:(t + 1) * TH], g["ones_fr"][0:1, :],
                                           invr[0:1, t * TH:(t + 1) * TH], start=True, stop=True)
                else:
                    ins = None
                w.inc("n_bc", ins)
                w.wait("DVE", "n_bc", w.cnt["n_bc"])
                ins = (w.eng.tensor_copy(invb[:, :], smp[:, :]) if w.on("DVE") else None)
                w.inc("n_cp", ins)
                w.wait_slack("DVE", "sst", 16)
                if w.on("DVE"):
                    for t in range(2):
                        ins = w.eng.tensor_mul(ctxs[h % 2][:, t * TH:(t + 1) * TH],
                                               cxp[h % 2][:, t * TH:(t + 1) * TH],
                                               invb[:, t * TH:(t + 1) * TH])
                else:
                    ins = None
                w.inc("n_dn", ins)
                w.wait("SP", "n_dn", w.cnt["n_dn"])
                ins = (w.eng.dma_start(out=ctx_d[128 * h:128 * (h + 1), :], in_=ctxs[h % 2][:, :])
                       if w.on("SP") else None)
                w.inc("sst", ins, 16)
            assert bias_idx == NBIAS, (bias_idx, NBIAS)

        run_phase(nc, sems, bases, walk_core)
    es_core.close()

    es_op = ExitStack()
    ctxr = es_op.enter_context(nc.sbuf_tensor("op_ctxr", [128, NCH * NT], BF16))
    wob = [es_op.enter_context(nc.sbuf_tensor(f"op_wo{i}", [128, 32 * 128], BF16)) for i in range(2)]
    xop = [es_op.enter_context(nc.sbuf_tensor(f"op_x{i}", [128, NT], F32)) for i in range(4)]
    x2s = [es_op.enter_context(nc.sbuf_tensor(f"op_o{i}", [128, NT], F32)) for i in range(2)]
    osq = [es_op.enter_context(nc.sbuf_tensor(f"op_sq{i}", [128, NT], F32)) for i in range(2)]

    with ExitStack() as eso:
        pps = [eso.enter_context(nc.psum_tensor(f"op_ps{i}", [128, TH], F32)) for i in range(4)]
        st_sum = eso.enter_context(nc.psum_tensor("op_stsum", [1, NT], F32))
        st_sq = eso.enter_context(nc.psum_tensor("op_stsq", [1, NT], F32))

        def emit_ostats(w, f):
            for th in range(2):
                cl = slice(th * TH, (th + 1) * TH)
                i = (w.eng.matmul(st_sum[0:1, cl], g["ones_fc"][:, 0:1], x2s[f % 2][:, cl],
                                  start=(f == 0), stop=(f == NCH - 1))
                     if w.on("PE") else None)
            w.inc("p_x", i)
            for th in range(2):
                cl = slice(th * TH, (th + 1) * TH)
                i = (w.eng.matmul(st_sq[0:1, cl], g["ones_fc"][:, 0:1], osq[f % 2][:, cl],
                                  start=(f == 0), stop=(f == NCH - 1))
                     if w.on("PE") else None)
            w.inc("p_sq", i)

        def walk_oproj(w):
            w.wait("SP", "sst", w.cnt["sst"])
            for c in range(NCH):
                i = (w.eng.dma_start(out=ctxr[:, c * NT:(c + 1) * NT],
                                     in_=ctx_d[128 * c:128 * (c + 1), :])
                     if w.on("SP") else None)
                w.inc("sx", i, 16)
            ctx_sx = w.cnt["sx"]
            for f in range(NCH):
                if f > 1:
                    w.wait_slack("PE", "a_sq", 1)
                    emit_ostats(w, f - 2)
                w.wait_slack("POOL", "p_o", 2)
                i = (w.eng.dma_start(out=wob[f % 2][:, :],
                                     in_=woutslab[f].rearrange("p cc h -> p (cc h)"))
                     if w.on("POOL") else None)
                w.inc("sw", i, 16)
                w.wait_slack("SP", "d_o", 6)
                i = (w.eng.dma_start(out=xop[f % 4][:, :], in_=x1_d[128 * f:128 * (f + 1), :])
                     if w.on("SP") else None)
                w.inc("sx", i, 16)
                w.wait("PE", "sw", w.cnt["sw"])
                if f == 0:
                    w.wait("PE", "sx", ctx_sx)
                for t in range(2):
                    slot = (f % 2) * 2 + t
                    w.wait_slack("PE", "d_o", 3)
                    if w.on("PE"):
                        for cc in range(NCH):
                            i = w.eng.matmul(pps[slot][:, :], wob[f % 2][:, 128 * cc:128 * (cc + 1)],
                                             ctxr[:, cc * NT + t * TH: cc * NT + (t + 1) * TH],
                                             start=(cc == 0), stop=(cc == NCH - 1))
                    else:
                        i = None
                    w.inc("p_o", i)
                    w.wait("DVE", "p_o", w.cnt["p_o"])
                    w.wait("DVE", "sx", w.cnt["sx"])
                    w.wait_slack("DVE", "sst", 16)
                    w.wait_slack("DVE", "p_x", 0)
                    i = (w.eng.scalar_tensor_tensor(
                        x2s[f % 2][:, t * TH:(t + 1) * TH], pps[slot][:, :],
                        g["boutp"][:, f:f + 1],
                        xop[f % 4][:, t * TH:(t + 1) * TH], ALU.add, ALU.add)
                        if w.on("DVE") else None)
                    w.inc("d_o", i)
                w.wait("ACT", "d_o", w.cnt["d_o"])
                w.wait_slack("ACT", "p_sq", 0)
                i = (w.eng.activation(osq[f % 2][:, :], x2s[f % 2][:, :], AF.Square)
                     if w.on("ACT") else None)
                w.inc("a_sq", i)
                w.wait("SP", "d_o", w.cnt["d_o"])
                i = (w.eng.dma_start(out=x2_d[128 * f:128 * (f + 1), :], in_=x2s[f % 2][:, :])
                     if w.on("SP") else None)
                w.inc("sst", i, 16)
            w.wait("PE", "a_sq", w.cnt["a_sq"])
            emit_ostats(w, NCH - 2)
            emit_ostats(w, NCH - 1)
            w.wait("DVE", "p_x", w.cnt["p_x"])
            w.wait("DVE", "p_sq", w.cnt["p_sq"])
            if w.on("DVE"):
                w.eng.tensor_copy(g["sumrow"][0:1, :], st_sum[0:1, :])
                i = w.eng.tensor_copy(g["sqrow"][0:1, :], st_sq[0:1, :])
            else:
                i = None
            w.inc("d_st", i)
        run_phase(nc, sems, bases, walk_oproj)
    es_op.close()
    es_att.close()

    emit_ffn(x2_d, xout, "f2", final=True)

    top.close()
    return nc


_NC_CACHE = {}


def kernel(**inputs):
    x = np.asarray(inputs["L0_attn_hidden_states"], np.float32)
    host = _host_arrays(inputs)

    if "nc" not in _NC_CACHE:
        _NC_CACHE["nc"] = build_nc()
    nc = _NC_CACHE["nc"]

    in_maps = []
    for c in range(NCORES):
        b, j = divmod(c, 4)
        jA, jB = j, 7 - j
        xa = x[b, jA * 512:(jA + 1) * 512, :]
        xb = x[b, jB * 512:(jB + 1) * 512, :]
        xin_c = np.ascontiguousarray(np.concatenate([xa, xb], axis=0).T)
        actbias, maskarr = _core_tables(j)
        m = {"xin": xin_c, "actbias": actbias, "maskarr": maskarr}
        m.update(host)
        in_maps.append(m)

    res = run_bass_kernel_spmd(nc, in_maps, core_ids=list(range(NCORES)))

    out = np.empty((B, S, D), np.float32)
    for c in range(NCORES):
        b, j = divmod(c, 4)
        jA, jB = j, 7 - j
        r = np.asarray(res.results[c]["xout"])
        out[b, jA * 512:(jA + 1) * 512, :] = r[:, 0:512].T
        out[b, jB * 512:(jB + 1) * 512, :] = r[:, 512:1024].T
    return out
